# version 14 (final)
# baseline (speedup 1.0000x reference)
import contextlib
import os
import numpy as np
import ml_dtypes

import concourse.bass as bass
import concourse.mybir as mybir
from concourse.bass_utils import run_bass_kernel_spmd
from concourse.aot_env import allow_ici_all_to_all

F32 = mybir.dt.float32
BF16 = mybir.dt.bfloat16
AF = mybir.ActivationFunctionType
ALU = mybir.AluOpType
AX = mybir.AxisListType
BF = ml_dtypes.bfloat16

D = 1024
T = 4096
DFF = 4096
TOK = 2048
NT = 16
MAGIC = float(np.float32(1.5 * 2 ** 23))
SS1 = 6 * 64 * 2048
SS2 = 2 * 64 * 2048
BLKE = 64 * 2048
K_ = 1024


class Sem:
    def __init__(self, h):
        self.h = h
        self.n = 0

    def inc(self, instr, k=1):
        instr.then_inc(self.h, k)
        self.n += k
        return self.n


def _epilogue(nc, blk, x_sl, y, OUT, CC=None):
    @blk.gpsimd
    def _(e):
        if CC is not None and CC.n:
            e.wait_ge(CC.h, CC.n)
        OUT.inc(e.dma_start(out=y[:, :], in_=x_sl[:, :]), 16)
        e.wait_ge(OUT.h, OUT.n)


def schunks(r):
    if r is None:
        return [(0, 512), (512, 1024)]
    out = [(128 * r, 128 * r + 128)]
    a = 128 * (r + 1)
    if a < 512:
        out += [(a, 512), (512, 1024)]
    elif a < 1024:
        out += [(a, 1024)]
    return out


PH = int(os.environ.get("CLAUDE_MK_PHASES", "999"))


def build(core):
    nc = bass.Bass("TRN2", target_bir_lowering=False, debug=False)

    x_sl = nc.dram_tensor("x_sl", [TOK, D], F32, kind="ExternalInput").ap()
    w_qkvT = nc.dram_tensor("w_qkvT", [D, 3 * D], F32, kind="ExternalInput").ap()
    w_projT = nc.dram_tensor("w_projT", [D, D], F32, kind="ExternalInput").ap()
    w_inT_sl = nc.dram_tensor("w_inT_sl", [512, D], F32, kind="ExternalInput").ap()
    w_out_sl = nc.dram_tensor("w_out_sl", [512, D], F32, kind="ExternalInput").ap()
    b_qkv_t = nc.dram_tensor("b_qkv_t", [128, 24], F32, kind="ExternalInput").ap()
    b_in_t = nc.dram_tensor("b_in_t", [128, 32], F32, kind="ExternalInput").ap()
    b_out_row = nc.dram_tensor("b_out_row", [1, D], F32, kind="ExternalInput").ap()
    b_proj_row = nc.dram_tensor("b_proj_row", [1, D], F32, kind="ExternalInput").ap()
    pi_sl = nc.dram_tensor("pi_sl", [1, 64], F32, kind="ExternalInput").ap()
    augq_d = nc.dram_tensor("augq", [64, T], BF16, kind="ExternalInput").ap()
    augk_d = nc.dram_tensor("augk", [64, T], BF16, kind="ExternalInput").ap()
    trimask_d = nc.dram_tensor("trimask", [128, 128], BF16, kind="ExternalInput").ap()
    ident_d = nc.dram_tensor("ident", [128, 128], BF16, kind="ExternalInput").ap()
    identf_d = nc.dram_tensor("identf", [128, 128], F32, kind="ExternalInput").ap()
    y = nc.dram_tensor("y", [TOK, D], F32, kind="ExternalOutput").ap()

    TAPS = os.environ.get("CLAUDE_MK_TAPS", "0") == "1"
    a2a1_in = nc.dram_tensor("a2a1_in", [8 * SS1], BF16).ap()
    a2a1_out = nc.dram_tensor("a2a1_out", [8 * SS1], BF16).ap()
    aga_in = nc.dram_tensor("aga_in", [9, 128], F32).ap()
    aga_out = nc.dram_tensor("aga_out", [72, 128], F32).ap()
    agw_in = nc.dram_tensor("agw_in", [1048576], BF16).ap()
    agw_out = nc.dram_tensor("agw_out", [8388608], BF16).ap()
    agh_in = nc.dram_tensor("agh_in", [1, 128], F32).ap()
    agh_out = nc.dram_tensor("agh_out", [8, 128], F32).ap()
    a2a2_in = nc.dram_tensor("a2a2_in", [8 * SS2], BF16).ap()
    a2a2_out = nc.dram_tensor("a2a2_out", [8 * SS2], BF16).ap()
    hid_dram = nc.dram_tensor("hid_dram", [DFF * TOK], F32).ap()
    hsim_dram = nc.dram_tensor("hsim_dram", [DFF * TOK], BF16).ap()
    xn_dram = nc.dram_tensor("xn_dram", [TOK * D], F32).ap()
    bscr = nc.dram_tensor("bscr", [4, 128], F32).ap()
    tout = {}
    if TAPS:
        for nm, src in (("t_xn", xn_dram), ("t_hid", hid_dram),
                        ("t_hsim", hsim_dram), ("t_a1", a2a1_out),
                        ("t_a2", a2a2_out), ("t_agw", agw_out)):
            tout[nm] = (nc.dram_tensor(nm, list(src.tensor.shape),
                                       src.tensor.dtype,
                                       kind="ExternalOutput").ap(), src)
        for nm, src in (("t_aga", aga_out), ("t_agh", agh_out),
                        ("t_bscr", bscr)):
            tout[nm] = (nc.dram_tensor(nm, list(src.tensor.shape),
                                       src.tensor.dtype,
                                       kind="ExternalOutput").ap(), src)

    es = contextlib.ExitStack()

    def sbt(name, shape, dt):
        return es.enter_context(nc.sbuf_tensor(name, shape, dt))

    ident = sbt("ident_sb", [128, 128], BF16)
    trimask = sbt("trimask_sb", [128, 128], BF16)
    onesb = sbt("onesb", [2, 128], BF16)
    bqkv = sbt("bqkv", [128, 24], F32)
    bq8 = sbt("bq8", [128, 8], F32)
    bv16 = sbt("bv16", [128, 8], BF16)
    bin_t = sbt("bin_t", [128, 32], F32)
    scl = sbt("scl_small", [128, 16], F32)
    amx_in = sbt("amx_in", [128, 4], F32)
    amx_out = sbt("amx_out", [128, 4], F32)
    sclq_in = sbt("sclq_in", [128, 4], F32)
    sclq_out = sbt("sclq_out", [128, 4], F32)
    invscl_in = sbt("invscl_in", [128, 32], F32)
    cs_t = sbt("cs_t", [128, 32], F32)
    gath_a = sbt("gath_a", [128, 32], F32)
    gath_b = sbt("gath_b", [128, 32], F32)
    gath_x = sbt("gath_x", [128, 8], F32)
    stat = sbt("stat", [128, 80], F32)
    mean_t = sbt("mean_t", [128, 16], F32)
    istd_t = sbt("istd_t", [128, 16], F32)
    mbias_t = sbt("mbias_t", [128, 16], F32)
    acc1 = sbt("acc1", [128, 1], F32)
    acc2 = sbt("acc2", [128, 1], F32)
    chrow = sbt("chrow", [128, 1], F32)
    row1 = sbt("row1", [1, 128], F32)
    row2 = sbt("row2", [1, 128], F32)
    onesr = sbt("onesr", [1, 128], F32)
    pirow = sbt("pirow", [1, 64], F32)
    bor = sbt("bor", [1, D], F32)
    bpr = sbt("bpr", [1, D], F32)
    pcr = sbt("pcr", [1, D], F32)
    hi_r = sbt("hi_r", [1, D], BF16)
    lo_r = sbt("lo_r", [1, D], BF16)
    rden = sbt("rden", [1, 1024], F32)
    cfix = sbt("cfix", [128, 8], F32)
    identf = sbt("identf_sb", [128, 128], F32)
    aga_sb = sbt("aga_sb", [72, 128], F32)
    hrow = sbt("hrow", [1, 1024], F32)
    rhi_r = sbt("rhi_r", [1, 1024], BF16)
    bcn = sbt("bcn", [64, 1024], F32)
    rlo_r = sbt("rlo_r", [1, 1024], BF16)
    wprojb = sbt("wprojb", [128, 8 * 1024], BF16)

    ARENA = 152 * K_
    arena = es.enter_context(nc.sbuf_tensor("arena", [128, ARENA], mybir.dt.uint8))
    abase = nc.lookup_mloc(arena).addr

    from concourse.tensor_handle import SBTensorHandle, PSumTensorHandle

    def placeP(name, shape, dt, bank):
        hnd = PSumTensorHandle(name, shape, dt)
        nc._tensor(hnd.name, hnd.shape, hnd.dtype, type="PSUM")
        m = nc.lookup_mloc(hnd)
        m.allocated = True
        m.bank = bank
        return hnd

    def place(name, shape, dt, off):
        nbytes = int(np.prod(shape[1:])) * mybir.dt.size(dt)
        hnd = SBTensorHandle(name, shape, dt,
                             manual_sbuf_range=(off, off + nbytes),
                             manual_base_name="arena")
        nc._tensor(hnd.name, hnd.shape, hnd.dtype, type="SB")
        mloc = nc.lookup_mloc(hnd)
        mloc.allocated = True
        mloc.addr = abase + off
        return hnd

    wqkvb = place("wqkvb", [128, 8 * 3072], BF16, 0)
    xnT = place("xnT", [128, 8 * 2048], BF16, 48 * K_)
    x_rb = [place(f"x_rb{i}", [128, D], F32, 80 * K_ + 4 * K_ * i) for i in range(4)]
    qtmp = [place(f"qtmp{i}", [128, D], F32, 96 * K_ + 4 * K_ * i) for i in range(2)]
    wfull = place("wfull", [128, 4 * D], F32, 104 * K_)
    wqT_q = place("wqT_q", [128, 4 * D], BF16, 120 * K_)
    woq_q = place("woq_q", [128, 4 * D], BF16, 128 * K_)
    xn_st = [place(f"xn_st{i}", [128, D], F32, 136 * K_ + 4 * K_ * i) for i in range(2)]
    xnb_st = [place(f"xnb_st{i}", [128, D], BF16, 144 * K_ + 2 * K_ * i) for i in range(2)]
    qk_st = [place(f"qk_st{i}", [128, 512], BF16, 80 * K_ + K_ * i) for i in range(4)]
    v_st = [place(f"v_st{i}", [128, 1024], BF16, 84 * K_ + 2 * K_ * i) for i in range(2)]
    wqin_st = [place(f"wqin_st{i}", [128, 8 * 128], BF16, 2 * K_ * i) for i in range(2)]
    xsimT = place("xsimT", [128, 8 * 2048], BF16, 4 * K_)
    xn_rb = [place(f"xn_rb{i}", [128, 2048], F32, 96 * K_ + 8 * K_ * i) for i in range(2)]
    xsq = [place(f"xsq{i}", [128, D], F32, 112 * K_ + 4 * K_ * i) for i in range(2)]
    xsb = [place(f"xsb{i}", [128, D], BF16, 120 * K_ + 2 * K_ * i) for i in range(2)]
    hidst = [place(f"hidst{i}", [128, 2048], F32, 124 * K_ + 8 * K_ * i) for i in range(2)]
    wqout = place("wqout", [128, 32 * 1024], BF16, 0)
    hrb = [place(f"hrb{i}", [128, 2048], F32, 64 * K_ + 8 * K_ * i) for i in range(2)]
    hqb = [place(f"hqb{i}", [128, 2048], BF16, 80 * K_ + 4 * K_ * i) for i in range(2)]
    qbuf = [place("qbuf0", [128, T], BF16, 88 * K_),
            place("qbuf1", [128, T], BF16, 109 * K_)]
    kbuf = [place("kbuf0", [128, T], BF16, 96 * K_),
            place("kbuf1", [128, T], BF16, 117 * K_)]
    vbuf = [place("vbuf0", [128, 2080], BF16, 104 * K_),
            place("vbuf1", [128, 2080], BF16, 125 * K_)]
    pbuf = [place(f"pbuf{i}", [128, 1024], BF16, 130 * K_ + 2 * K_ * i) for i in range(2)]
    anst = [place(f"anst{i}", [64, 1024], BF16, 134 * K_ + 2 * K_ * i) for i in range(2)]
    s_tok = place("s_tok", [128, 16 * 1024], BF16, 0)
    x_rb2 = [place(f"x_rb2{i}", [128, D], F32, 32 * K_ + 4 * K_ * i) for i in range(4)]
    ostage = [place(f"ostage{i}", [128, D], F32, 48 * K_ + 4 * K_ * i) for i in range(2)]
    g_tok = place("g_tok", [128, 16 * 1024], BF16, 88 * K_)
    hs_t = [place(f"hs_t{i}", [128, 32 * 128], BF16, 120 * K_ + 8 * K_ * i) for i in range(2)]
    bo_bc = place("bo_bc", [128, D], F32, 136 * K_)
    pc_bc = place("pc_bc", [128, D], F32, 140 * K_)
    attT = ([place(f"attT{i}", [128, 2048], BF16, 144 * K_ + 4 * K_ * i) for i in range(2)]
            + [place(f"attT{i + 2}", [128, 2048], BF16, 64 * K_ + 4 * K_ * i)
               for i in range(6)])

    comp_names = ["CC", "VW1", "SW1", "VW2", "SW2", "VW3", "TWT", "VWT", "SSQ",
                  "VRD", "VST", "SIS", "VMB", "SLN", "VCA", "TTR", "VTR", "VAX",
                  "TQK", "SEV", "TVV", "SVV", "SXQ", "VXQ", "TXS", "VXS", "TF1",
                  "SE1", "VAM", "TSS", "SPP", "VMM", "TAV", "VN1", "TBB", "VN2",
                  "TPD", "VH", "TT2", "VE2", "SSI", "TPJ", "VPJ", "VO", "OUT",
                  "VBC", "TBC", "VDN", "TGA", "HZ"]
    dma_names = ["LDC", "GDC", "GDS", "SDW", "SAGW0", "SAGW1",
                 "LDX0", "LDX1", "LDX2", "LDX3", "GDD", "SPX0", "SPX1",
                 "SQK0", "SQK1", "SQK2", "SQK3", "SV0", "SV1", "LDA1",
                 "LDR0", "LDR1", "LDQ0", "LDQ1", "SPL0", "SPL1", "LPF0",
                 "LPF1", "LWQ", "LDG", "LDH0", "LDH1", "GHS0", "GHS1",
                 "GA20", "GA21", "LHS0", "LHS1", "LAT", "SO0", "SO1"]
    q = {n: Sem(es.enter_context(nc.semaphore(n))) for n in comp_names + dma_names}
    (CC, VW1, SW1, VW2, SW2, VW3, TWT, VWT, SSQ, VRD, VST, SIS, VMB, SLN,
     VCA, TTR, VTR, VAX, TQK, SEV, TVV, SVV, SXQ, VXQ, TXS, VXS, TF1, SE1,
     VAM, TSS, SPP, VMM, TAV, VN1, TBB, VN2, TPD, VH, TT2, VE2, SSI, TPJ,
     VPJ, VO, OUT, VBC, TBC, VDN, TGA, HZ) = (q[n] for n in comp_names)
    LDC, GDC, GDS, SDW = q["LDC"], q["GDC"], q["GDS"], q["SDW"]
    SAGW = [q["SAGW0"], q["SAGW1"]]
    LDX = [q[f"LDX{i}"] for i in range(4)]
    GDD = q["GDD"]
    SPX = [q["SPX0"], q["SPX1"]]
    SQK = [q[f"SQK{i}"] for i in range(4)]
    SV = [q["SV0"], q["SV1"]]
    LDA1 = q["LDA1"]
    LDR = [q["LDR0"], q["LDR1"]]
    LDQ = [q["LDQ0"], q["LDQ1"]]
    SPL = [q["SPL0"], q["SPL1"]]
    LPF = [q["LPF0"], q["LPF1"]]
    LWQ, LDG = q["LWQ"], q["LDG"]
    LDH = [q["LDH0"], q["LDH1"]]
    GHS = [q["GHS0"], q["GHS1"]]
    GA2 = [q["GA20"], q["GA21"]]
    LHS = [q["LHS0"], q["LHS1"]]
    LAT = q["LAT"]
    SO = [q["SO0"], q["SO1"]]

    CONSTS = [MAGIC, -36.0, 0.0, 1.0, 0.125]
    for i, v in enumerate(CONSTS):
        nc.const_aps.register(F32, float(v), cfix[:, i:i + 1])

    blk = es.enter_context(nc.Block())
    snap = {}

    class _RecInstr:
        __slots__ = ("incs",)

        def __init__(self):
            self.incs = []

        def then_inc(self, sem, k=1):
            self.incs.append((sem, k))
            return self

    class _RecEngine:
        def __init__(self, calls):
            self._calls = calls

        def __getattr__(self, name):
            calls = self._calls

            def f(*a, **kw):
                ri = _RecInstr()
                calls.append((name, a, kw, ri))
                return ri
            return f

    _reccalls = {n: [] for n in ("sync", "scalar", "vector", "tensor", "gpsimd")}
    _recs = {n: _RecEngine(_reccalls[n]) for n in _reccalls}

    class _SecProxy:
        def __init__(self):
            self.sync = lambda fn: fn(_recs["sync"])
            self.scalar = lambda fn: fn(_recs["scalar"])
            self.vector = lambda fn: fn(_recs["vector"])
            self.tensor = lambda fn: fn(_recs["tensor"])
            self.gpsimd = lambda fn: fn(_recs["gpsimd"])

    _realblk = blk
    blk = _SecProxy()

    def _finalize():
        with allow_ici_all_to_all():
            for name in ("sync", "scalar", "vector", "tensor", "gpsimd"):
                def emit(e, name=name):
                    for (m, a, kw, ri) in _reccalls[name]:
                        instr = getattr(e, m)(*a, **kw)
                        for (sem, k) in ri.incs:
                            instr.then_inc(sem, k)
                getattr(_realblk, name)(emit)

    psW = [placeP(f"psW{i}", [128, 1024], BF16, i) for i in range(2)]
    psT = [placeP(f"psT{i}", [128, 1024], BF16, 2 + i) for i in range(2)]
    ps2 = [placeP(f"ps2_{i}", [128, 512], F32, 4 + i) for i in range(4)]
    psX = [placeP(f"psX{i}", [128, 1024], BF16, i) for i in range(2)]
    psF1 = [placeP(f"psF1_{i}", [128, 512], F32, i) for i in range(8)]
    ps_s = [placeP(f"ps_s{i}", [128, 1024], F32, 2 * i) for i in range(2)]
    psO = placeP("psO", [65, 1024], F32, 4)
    psBC = placeP("psBC", [64, 1024], F32, 6)
    psF = [placeP(f"psF_{i}", [128, 512], F32, i) for i in range(4)]
    psB = [placeP(f"psB_{i}", [128, 512], F32, 4 + i) for i in range(2)]

    @blk.sync
    def _(e):
        LDC.inc(e.dma_start(out=ident[:], in_=ident_d[:, :]), 16)
        LDC.inc(e.dma_start(out=identf[:], in_=identf_d[:, :]), 16)
        LDC.inc(e.dma_start(out=trimask[:], in_=trimask_d[:, :]), 16)
        LDC.inc(e.dma_start(out=bqkv[:], in_=b_qkv_t[:, :]), 16)
        LDC.inc(e.dma_start(out=bin_t[:], in_=b_in_t[:, :]), 16)
        LDC.inc(e.dma_start(out=bor[:], in_=b_out_row[:, :]), 16)
        LDC.inc(e.dma_start(out=bpr[:], in_=b_proj_row[:, :]), 16)
        LDC.inc(e.dma_start(out=pirow[:], in_=pi_sl[:, :]), 16)
        LDC.inc(e.dma_start(out=wfull.rearrange("p (c d) -> p c d", d=D),
                            in_=w_inT_sl.rearrange("(c p) d -> p c d", p=128)), 16)
        snap["win"] = LDC.n

    @blk.gpsimd
    def _(e):
        GDC.inc(e.dma_start(out=wqkvb.rearrange("p (c f) -> p c f", f=3072),
                            in_=w_qkvT.rearrange("(c p) f -> p c f", p=128)), 16)
        GDC.inc(e.dma_start(out=wprojb.rearrange("p (c f) -> p c f", f=1024),
                            in_=w_projT.rearrange("(c p) f -> p c f", p=128)), 16)

    if PH < 11:
        _epilogue(nc, blk, x_sl, y, OUT, CC)
        _finalize()
        es.close()
        return nc

    def hz(e, ins):
        HZ.inc(ins)
        e.wait_ge(HZ.h, HZ.n)
        return ins

    @blk.vector
    def _(e):
        e.wait_ge(LDC.h, snap["win"])
        for i, v in enumerate(CONSTS):
            e.memset(cfix[:, i:i + 1], float(v))
        e.memset(onesr[:], 1.0)
        e.memset(onesb[:], 1.0)
        e.tensor_scalar(out=bq8[:], in0=bqkv[:, 0:8], scalar1=0.125, scalar2=None,
                        op0=ALU.mult)
        e.tensor_copy(out=bv16[:], in_=bqkv[:, 16:24])
        for c4 in range(4):
            ins = e.tensor_reduce(out=amx_in[:, c4:c4 + 1],
                                  in_=wfull[:, c4 * D:(c4 + 1) * D],
                                  axis=AX.X, op=ALU.max, apply_absolute_value=True)
        hz(e, ins)
        hz(e, e.tensor_scalar(out=amx_in[:], in0=amx_in[:], scalar1=1e-8,
                              scalar2=None, op0=ALU.max))
        hz(e, e.reciprocal(out=sclq_in[:], in_=amx_in[:]))
        VW1.inc(e.tensor_scalar(out=sclq_in[:], in0=sclq_in[:], scalar1=127.0,
                                scalar2=None, op0=ALU.mult))

    @blk.scalar
    def _(e):
        e.wait_ge(VW1.h, 1)
        for c4 in range(4):
            if c4 >= 2:
                e.wait_ge(VW2.h, c4 - 1)
            SW1.inc(e.activation(out=qtmp[c4 % 2][:], in_=wfull[:, c4 * D:(c4 + 1) * D],
                                 func=AF.Identity, scale=sclq_in[:, c4:c4 + 1],
                                 bias=MAGIC))

    @blk.vector
    def _(e):
        for c4 in range(4):
            e.wait_ge(SW1.h, c4 + 1)
            VW2.inc(e.tensor_scalar(out=wqT_q[:, c4 * D:(c4 + 1) * D],
                                    in0=qtmp[c4 % 2][:], scalar1=MAGIC, scalar2=None,
                                    op0=ALU.subtract))

    @blk.sync
    def _(e):
        e.wait_ge(SW1.h, 4)
        LDC.inc(e.dma_start(out=wfull.rearrange("p (c d) -> p c d", d=D),
                            in_=w_out_sl.rearrange("(c p) d -> p c d", p=128)), 16)
        snap["wout"] = LDC.n

    @blk.vector
    def _(e):
        e.wait_ge(LDC.h, snap["wout"])
        for c4 in range(4):
            ins = e.tensor_reduce(out=amx_out[:, c4:c4 + 1],
                                  in_=wfull[:, c4 * D:(c4 + 1) * D],
                                  axis=AX.X, op=ALU.max, apply_absolute_value=True)
        hz(e, ins)
        hz(e, e.tensor_scalar(out=amx_out[:], in0=amx_out[:], scalar1=1e-8,
                              scalar2=None, op0=ALU.max))
        hz(e, e.reciprocal(out=sclq_out[:], in_=amx_out[:]))
        VW1.inc(e.tensor_scalar(out=sclq_out[:], in0=sclq_out[:], scalar1=127.0,
                                scalar2=None, op0=ALU.mult))

    @blk.scalar
    def _(e):
        e.wait_ge(VW1.h, 2)
        for c4 in range(4):
            if c4 >= 2:
                e.wait_ge(VW3.h, c4 - 1)
            SW2.inc(e.activation(out=qtmp[c4 % 2][:], in_=wfull[:, c4 * D:(c4 + 1) * D],
                                 func=AF.Identity, scale=sclq_out[:, c4:c4 + 1],
                                 bias=MAGIC))

    @blk.vector
    def _(e):
        for c4 in range(4):
            e.wait_ge(SW2.h, c4 + 1)
            VW3.inc(e.tensor_scalar(out=woq_q[:, c4 * D:(c4 + 1) * D],
                                    in0=qtmp[c4 % 2][:], scalar1=MAGIC, scalar2=None,
                                    op0=ALU.subtract))

    if PH < 12:
        _epilogue(nc, blk, x_sl, y, OUT, CC)
        _finalize()
        es.close()
        return nc

    @blk.tensor
    def _(e):
        e.wait_ge(LDC.h, snap["win"])
        e.wait_ge(VW2.h, 4)
        for j in range(8):
            par = j % 2
            if j >= 2:
                e.wait_ge(VWT.h, j - 1)
            for c4 in range(4):
                ins = e.transpose(
                    psW[par][:, c4 * 128:(c4 + 1) * 128],
                    wqT_q[:, c4 * D + j * 128:c4 * D + (j + 1) * 128], ident[:])
                if c4 == 3:
                    TWT.inc(ins)

    @blk.vector
    def _(e):
        for j in range(8):
            par = j % 2
            e.wait_ge(TWT.h, j + 1)
            if j >= 2:
                e.wait_ge(SAGW[par].h, 16 * (j // 2))
            VWT.inc(e.tensor_copy(out=xnb_st[par][:, 0:512],
                                  in_=psW[par][:, 0:512]))

    @blk.scalar
    def _(e):
        for j in range(8):
            par = j % 2
            e.wait_ge(VWT.h, j + 1)
            dst = agw_in[j * 65536:(j + 1) * 65536].rearrange("(p qq) -> p qq", qq=512)
            SAGW[par].inc(e.dma_start(out=dst, in_=xnb_st[par][:, 0:512]), 16)
        e.wait_ge(VW3.h, 4)
        dst = agw_in[524288:1048576].rearrange("(c p qq) -> p c qq", c=4, p=128)
        SDW.inc(e.dma_start(out=dst, in_=woq_q.rearrange("p (c qq) -> p c qq", qq=1024)),
                16)

    @blk.gpsimd
    def _(e):
        e.wait_ge(VW1.h, 2)
        GDS.inc(e.dma_start(out=aga_in[0:4, :].rearrange("r p -> p r"), in_=amx_in[:],
                            allow_slow_non_contiguous=True), 16)
        GDS.inc(e.dma_start(out=aga_in[4:8, :].rearrange("r p -> p r"), in_=amx_out[:],
                            allow_slow_non_contiguous=True), 16)
        e.wait_ge(SAGW[0].h, 64)
        e.wait_ge(SAGW[1].h, 64)
        e.wait_ge(SDW.h, 16)
        CC.inc(e.collective_compute("AllGather", ALU.bypass,
                                    replica_groups=[list(range(8))],
                                    ins=[agw_in.opt()], outs=[agw_out.opt()]))

    if PH < 13:
        _epilogue(nc, blk, x_sl, y, OUT, CC)
        _finalize()
        es.close()
        return nc

    @blk.sync
    def _(e):
        for ti in range(NT):
            if ti >= 4:
                e.wait_ge(SSQ.h, ti - 3)
                e.wait_ge(VRD.h, ti - 3)
            LDX[ti % 4].inc(e.dma_start(out=x_rb[ti % 4][:],
                                        in_=x_sl[ti * 128:(ti + 1) * 128, :]), 16)

    @blk.scalar
    def _(e):
        e.wait_ge(SW2.h, 4)
        e.wait_ge(VW3.h, 4)
        for ti in range(NT):
            e.wait_ge(LDX[ti % 4].h, 16 * (ti // 4 + 1))
            SSQ.inc(e.activation(out=qtmp[0][:], in_=x_rb[ti % 4][:], func=AF.Square,
                                 accum_out=stat[:, 16 + ti:17 + ti]))
            e.wait_ge(SSQ.h, SSQ.n)

    @blk.vector
    def _(e):
        for ti in range(NT):
            e.wait_ge(LDX[ti % 4].h, 16 * (ti // 4 + 1))
            e.tensor_reduce(out=stat[:, ti:ti + 1], in_=x_rb[ti % 4][:], axis=AX.X,
                            op=ALU.add)
            e.tensor_reduce(out=stat[:, 32 + ti:33 + ti], in_=x_rb[ti % 4][:],
                            axis=AX.X, op=ALU.max)
            VRD.inc(e.tensor_reduce(out=stat[:, 48 + ti:49 + ti],
                                    in_=x_rb[ti % 4][:], axis=AX.X, op=ALU.min))
        e.wait_ge(VRD.h, 16)
        e.wait_ge(SSQ.h, 16)
        inv_d = 1.0 / 1024.0
        hz(e, e.tensor_scalar(out=mean_t[:], in0=stat[:, 0:16], scalar1=inv_d,
                              scalar2=None, op0=ALU.mult))
        hz(e, e.tensor_tensor(out=stat[:, 64:80], in0=mean_t[:], in1=mean_t[:],
                              op=ALU.mult))
        hz(e, e.tensor_scalar(out=istd_t[:], in0=stat[:, 16:32], scalar1=inv_d,
                              scalar2=None, op0=ALU.mult))
        hz(e, e.tensor_tensor(out=istd_t[:], in0=istd_t[:], in1=stat[:, 64:80],
                              op=ALU.subtract))
        hz(e, e.tensor_scalar(out=istd_t[:], in0=istd_t[:], scalar1=1e-5,
                              scalar2=None, op0=ALU.add))
        VST.inc(e.reciprocal(out=stat[:, 64:80], in_=istd_t[:]))

    @blk.scalar
    def _(e):
        e.wait_ge(VST.h, 1)
        SIS.inc(e.activation(out=istd_t[:], in_=stat[:, 64:80], func=AF.Sqrt))

    @blk.vector
    def _(e):
        e.wait_ge(SIS.h, 1)
        hz(e, e.tensor_tensor(out=mbias_t[:], in0=mean_t[:], in1=istd_t[:],
                              op=ALU.mult))
        e.tensor_scalar(out=mbias_t[:], in0=mbias_t[:], scalar1=-1.0, scalar2=None,
                        op0=ALU.mult)
        hz(e, e.tensor_tensor(out=stat[:, 64:80], in0=stat[:, 32:48], in1=mean_t[:],
                              op=ALU.subtract))
        hz(e, e.tensor_tensor(out=stat[:, 32:48], in0=mean_t[:], in1=stat[:, 48:64],
                              op=ALU.subtract))
        hz(e, e.tensor_tensor(out=stat[:, 64:80], in0=stat[:, 64:80],
                              in1=stat[:, 32:48], op=ALU.max))
        hz(e, e.tensor_tensor(out=stat[:, 64:80], in0=stat[:, 64:80], in1=istd_t[:],
                              op=ALU.mult))
        hz(e, e.tensor_reduce(out=acc1[:], in_=stat[:, 64:80], axis=AX.X,
                              op=ALU.max))
        hz(e, e.tensor_reduce(out=row2[0:1, 0:1], in_=pirow[:], axis=AX.X,
                              op=ALU.add))
        VMB.inc(e.tensor_scalar(out=row1[:], in0=onesr[:], scalar1=row2[0:1, 0:1],
                                scalar2=None, op0=ALU.mult))

    if PH < 14:
        _epilogue(nc, blk, x_sl, y, OUT, CC)
        _finalize()
        es.close()
        return nc

    @blk.gpsimd
    def _(e):
        e.wait_ge(VMB.h, 1)
        GDD.inc(e.dma_start(out=bscr[1:2, :], in_=row1[:]), 16)
        GDD.inc(e.dma_start(out=bscr[0:1, :].rearrange("a p -> p a"), in_=acc1[:],
                            allow_slow_non_contiguous=True), 16)
        e.wait_ge(GDD.h, GDD.n)
        GDD.inc(e.dma_start(out=chrow[:], in_=bscr[1:2, :].rearrange("a p -> p a"),
                            allow_slow_non_contiguous=True), 16)
        GDD.inc(e.dma_start(out=row2[:], in_=bscr[0:1, :]), 16)
        snap["gd_d2"] = GDD.n

    if PH < 15:
        _epilogue(nc, blk, x_sl, y, OUT, CC)
        _finalize()
        es.close()
        return nc

    @blk.vector
    def _(e):
        e.wait_ge(GDD.h, snap["gd_d2"])
        HZ.inc(e.tensor_reduce(out=row1[0:1, 0:1], in_=row2[:], axis=AX.X, op=ALU.max))
        e.wait_ge(HZ.h, HZ.n)
        VAX.inc(e.tensor_scalar(out=row1[:], in0=onesr[:], scalar1=row1[0:1, 0:1],
                                scalar2=None, op0=ALU.mult))

    if PH < 16:
        _epilogue(nc, blk, x_sl, y, OUT, CC)
        _finalize()
        es.close()
        return nc

    @blk.gpsimd
    def _(e):
        e.wait_ge(CC.h, 1)
        e.wait_ge(VAX.h, 1)
        e.wait_ge(GDS.h, 32)
        GDD.inc(e.dma_start(out=aga_in[8:9, :], in_=row1[:]), 16)
        e.wait_ge(GDD.h, GDD.n)
        CC.inc(e.collective_compute("AllGather", ALU.bypass,
                                    replica_groups=[list(range(8))],
                                    ins=[aga_in.opt()], outs=[aga_out.opt()]))

    if PH < 17:
        _epilogue(nc, blk, x_sl, y, OUT, CC)
        _finalize()
        es.close()
        return nc

    @blk.sync
    def _(e):
        e.wait_ge(SSQ.h, 16)
        e.wait_ge(VRD.h, 16)
        for ti in range(NT):
            if ti >= 4:
                e.wait_ge(SLN.h, ti - 3)
            LDX[ti % 4].inc(e.dma_start(out=x_rb[ti % 4][:],
                                        in_=x_sl[ti * 128:(ti + 1) * 128, :]), 16)

    @blk.scalar
    def _(e):
        e.wait_ge(SIS.h, 1)
        e.wait_ge(VMB.h, 1)
        for ti in range(NT):
            par = ti % 2
            e.wait_ge(LDX[ti % 4].h, 16 * (4 + ti // 4 + 1))
            if ti >= 2:
                e.wait_ge(VCA.h, ti - 1)
                e.wait_ge(SPX[par].h, 16 * (ti // 2))
            SLN.inc(e.activation(out=xn_st[par][:], in_=x_rb[ti % 4][:],
                                 func=AF.Identity, scale=istd_t[:, ti:ti + 1],
                                 bias=mbias_t[:, ti:ti + 1]))
            e.wait_ge(SLN.h, SLN.n)
            dst = xn_dram[ti * 128 * D:(ti + 1) * 128 * D].rearrange(
                "(p dd) -> p dd", dd=D)
            SPX[par].inc(e.dma_start(out=dst, in_=xn_st[par][:]), 16)

    if PH < 18:
        _epilogue(nc, blk, x_sl, y, OUT, CC)
        _finalize()
        es.close()
        return nc

    def _vca_one(e, ti):
        par = ti % 2
        e.wait_ge(SLN.h, ti + 1)
        if ti >= 2:
            e.wait_ge(TTR.h, ti - 1)
        VCA.inc(e.tensor_copy(out=xnb_st[par][:], in_=xn_st[par][:]))

    def _vtr_one(e, ti):
        par = ti % 2
        x3 = xnT.rearrange("p (j t) -> p j t", t=2048)
        e.wait_ge(TTR.h, ti + 1)
        psT3 = psT[par].rearrange("p (jj cc) -> p jj cc", cc=128)
        VTR.inc(e.tensor_copy(
            out=x3[:, :, ti * 128:(ti + 1) * 128],
            in_=psT3[:, :, :]))

    @blk.vector
    def _(e):
        e.wait_ge(SAGW[0].h, 64)
        e.wait_ge(SAGW[1].h, 64)
        for ti in range(NT):
            _vca_one(e, ti)
            if ti >= 1:
                _vtr_one(e, ti - 1)
        _vtr_one(e, NT - 1)

    if PH < 19:
        _epilogue(nc, blk, x_sl, y, OUT, CC)
        _finalize()
        es.close()
        return nc

    @blk.tensor
    def _(e):
        for ti in range(NT):
            par = ti % 2
            e.wait_ge(VCA.h, ti + 1)
            if ti >= 2:
                e.wait_ge(VTR.h, ti - 1)
            for j in range(8):
                ins = e.transpose(psT[par][:, j * 128:(j + 1) * 128],
                                  xnb_st[par][:, j * 128:(j + 1) * 128], ident[:])
                if j == 7:
                    TTR.inc(ins)

    if PH < 20:
        _epilogue(nc, blk, x_sl, y, OUT, CC)
        _finalize()
        es.close()
        return nc

    if PH < 21:
        _epilogue(nc, blk, x_sl, y, OUT, CC)
        _finalize()
        es.close()
        return nc

    @blk.tensor
    def _(e):
        e.wait_ge(GDC.h, 32)
        e.wait_ge(VTR.h, 16)
        e.wait_ge(VWT.h, 8)
        for fj in range(16):
            if fj >= 1:
                e.wait_ge(SEV.h, 4 * fj)
            for dk in range(8):
                for tj in range(4):
                    ins = e.matmul(
                        ps2[tj][:, :],
                        wqkvb[:, dk * 3072 + fj * 128:dk * 3072 + (fj + 1) * 128],
                        xnT[:, dk * 2048 + tj * 512:dk * 2048 + (tj + 1) * 512],
                        start=(dk == 0), stop=(dk == 7))
                    if dk == 7:
                        TQK.inc(ins)

    @blk.scalar
    def _(e):
        for fj in range(16):
            for tj in range(4):
                e.wait_ge(TQK.h, 4 * fj + tj + 1)
                st = qk_st[tj]
                if fj >= 1:
                    e.wait_ge(SQK[tj].h, 32 * fj)
                if fj < 8:
                    ins = e.activation(out=st[:], in_=ps2[tj][:, :], func=AF.Identity,
                                       scale=0.125, bias=bq8[:, fj:fj + 1])
                else:
                    ins = e.activation(out=st[:], in_=ps2[tj][:, :], func=AF.Identity,
                                       bias=bqkv[:, fj:fj + 1])
                SEV.inc(ins)
                e.wait_ge(SEV.h, SEV.n)
                for half in range(2):
                    if fj < 8:
                        hh = 2 * fj + half
                        base = (hh % 8) * SS1 + (hh // 8) * BLKE
                    else:
                        hh = 2 * (fj - 8) + half
                        base = (hh % 8) * SS1 + 2 * BLKE + (hh // 8) * BLKE
                    dst = a2a1_in[base:base + BLKE].rearrange("(dd t) -> dd t", t=2048)
                    SQK[tj].inc(e.dma_start(out=dst[:, tj * 512:(tj + 1) * 512],
                                            in_=st[64 * half:64 * half + 64, :]), 16)

    @blk.tensor
    def _(e):
        e.wait_ge(SEV.h, 64)
        for ti in range(NT):
            par = ti % 2
            banks = (ps2[2], ps2[3]) if par == 0 else (ps2[0], ps2[1])
            if ti >= 2:
                e.wait_ge(SVV.h, 2 * (ti - 1))
            for dk in range(8):
                for half in range(2):
                    ins = e.matmul(
                        banks[half][:, :],
                        xnT[:, dk * 2048 + ti * 128:dk * 2048 + (ti + 1) * 128],
                        wqkvb[:, dk * 3072 + 2048 + half * 512:
                              dk * 3072 + 2048 + (half + 1) * 512],
                        start=(dk == 0), stop=(dk == 7))
                    if dk == 7:
                        TVV.inc(ins)

    @blk.scalar
    def _(e):
        for ti in range(NT):
            par = ti % 2
            banks = (ps2[2], ps2[3]) if par == 0 else (ps2[0], ps2[1])
            if ti >= 2:
                e.wait_ge(SV[par].h, 32 * (ti // 2))
            for half in range(2):
                e.wait_ge(TVV.h, 2 * ti + half + 1)
                SVV.inc(e.activation(out=v_st[par][:, half * 512:(half + 1) * 512],
                                     in_=banks[half][:, :], func=AF.Identity))
            e.wait_ge(SVV.h, SVV.n)
            ap1 = a2a1_in.rearrange("(j xx) -> xx j", j=8)
            for half in range(2):
                vbase = 4 * BLKE + half * BLKE
                ap2 = ap1[vbase:vbase + BLKE, :].rearrange("(t dd) j -> t j dd", dd=64)
                src = v_st[par][:, half * 512:(half + 1) * 512].rearrange(
                    "p (j dd) -> p j dd", dd=64)
                SV[par].inc(e.dma_start(out=ap2[ti * 128:(ti + 1) * 128, :, :],
                                        in_=src), 16)

    @blk.gpsimd
    def _(e):
        for tj in range(4):
            e.wait_ge(SQK[tj].h, 32 * 16)
        e.wait_ge(SV[0].h, 32 * 8)
        e.wait_ge(SV[1].h, 32 * 8)
        e.wait_ge(CC.h, 2)
        with allow_ici_all_to_all():
            CC.inc(e.collective_compute("AllToAll", ALU.bypass,
                                        replica_groups=[list(range(8))],
                                        ins=[a2a1_in.opt()], outs=[a2a1_out.opt()]))

    if PH < 22:
        _epilogue(nc, blk, x_sl, y, OUT, CC)
        _finalize()
        es.close()
        return nc

    @blk.sync
    def _(e):
        e.wait_ge(CC.h, 2)
        LDA1.inc(e.dma_start(out=aga_sb[:], in_=aga_out[:, :]), 16)

    @blk.vector
    def _(e):
        e.wait_ge(TGA.h, 1)
        trv = ps2[2][:, 0:72].rearrange("p (r n) -> p r n", n=9)
        e.tensor_copy(out=gath_a.rearrange("p (r c) -> p r c", c=4), in_=trv[:, :, 0:4])
        e.tensor_copy(out=gath_b.rearrange("p (r c) -> p r c", c=4), in_=trv[:, :, 4:8])
        hz(e, e.tensor_copy(out=gath_x.rearrange("p (r c) -> p r c", c=1),
                            in_=trv[:, :, 8:9]))
        VDN.inc(e.nop())
        hz(e, e.tensor_reduce(out=scl[:, 0:1], in_=gath_x[:], axis=AX.X, op=ALU.max))
        hz(e, e.tensor_scalar(out=scl[:, 0:1], in0=scl[:, 0:1], scalar1=1e-8,
                              scalar2=None, op0=ALU.max))
        hz(e, e.reciprocal(out=scl[:, 1:2], in_=scl[:, 0:1]))
        hz(e, e.tensor_scalar(out=scl[:, 1:2], in0=scl[:, 1:2], scalar1=32767.0,
                              scalar2=None, op0=ALU.mult))
        hz(e, e.reciprocal(out=scl[:, 2:3], in_=scl[:, 1:2]))
        hz(e, e.tensor_scalar(out=gath_a[:], in0=gath_a[:], scalar1=1e-8,
                              scalar2=None, op0=ALU.max))
        VBC.inc(e.tensor_scalar(out=invscl_in[:], in0=gath_a[:],
                                scalar1=float(np.float32(1.0) / np.float32(127.0)),
                                scalar2=None, op0=ALU.mult))

    @blk.sync
    def _(e):
        e.wait_ge(SSQ.h, 16)
        e.wait_ge(VW3.h, 4)
        for ti in range(NT):
            par = ti % 2
            e.wait_ge(SPX[par].h, 16 * (ti // 2 + 1))
            if ti >= 2:
                e.wait_ge(SXQ.h, ti - 1)
            src = xn_dram[ti * 128 * D:(ti + 1) * 128 * D].rearrange(
                "(p dd) -> p dd", dd=D)
            LDR[par].inc(e.dma_start(out=xn_rb[par][:, 0:D], in_=src), 16)

    @blk.scalar
    def _(e):
        e.wait_ge(VBC.h, 1)
        for ti in range(NT):
            par = ti % 2
            e.wait_ge(LDR[par].h, 16 * (ti // 2 + 1))
            if ti >= 2:
                e.wait_ge(VXQ.h, ti - 1)
            SXQ.inc(e.activation(out=xsq[par][:], in_=xn_rb[par][:, 0:D],
                                 func=AF.Identity, scale=scl[:, 1:2], bias=MAGIC))

    def _vxq_one(e, ti):
        e.wait_ge(SXQ.h, ti + 1)
        if ti >= 2:
            e.wait_ge(TXS.h, ti - 1)
        VXQ.inc(e.tensor_scalar(out=xsb[ti % 2][:], in0=xsq[ti % 2][:],
                                scalar1=MAGIC, scalar2=scl[:, 2:3],
                                op0=ALU.subtract, op1=ALU.mult))

    def _vxs_one(e, ti):
        par = ti % 2
        xs3 = xsimT.rearrange("p (j t) -> p j t", t=2048)
        e.wait_ge(TXS.h, ti + 1)
        psX3 = psX[par].rearrange("p (jj cc) -> p jj cc", cc=128)
        VXS.inc(e.tensor_copy(
            out=xs3[:, :, ti * 128:(ti + 1) * 128],
            in_=psX3[:, :, :]))

    @blk.vector
    def _(e):
        for ti in range(NT):
            _vxq_one(e, ti)
            if ti >= 1:
                _vxs_one(e, ti - 1)
        _vxs_one(e, NT - 1)

    @blk.tensor
    def _(e):
        e.wait_ge(LDA1.h, 16)
        e.wait_ge(SEV.h, 64)
        e.wait_ge(SVV.h, 32)
        TGA.inc(e.transpose(ps2[2][:, 0:72], aga_sb[:], identf[0:72, 0:72]))

    @blk.tensor
    def _(e):
        e.wait_ge(VWT.h, 8)
        e.wait_ge(VDN.h, 1)
        for ti in range(NT):
            par = ti % 2
            e.wait_ge(VXQ.h, ti + 1)
            if ti >= 2:
                e.wait_ge(VXS.h, ti - 1)
            for j in range(8):
                ins = e.transpose(psX[par][:, j * 128:(j + 1) * 128],
                                  xsb[ti % 2][:, j * 128:(j + 1) * 128], ident[:])
                if j == 7:
                    TXS.inc(ins)

    @blk.sync
    def _(e):
        e.wait_ge(CC.h, 1)
        e.wait_ge(TVV.h, 32)
        for fj in range(32):
            par = fj % 2
            if fj >= 2:
                e.wait_ge(TF1.h, fj - 1)
            r = fj // 4
            blk = agw_out[r * 1048576:r * 1048576 + 524288].rearrange(
                "(j p qq) -> p j qq", j=8, p=128, qq=512)
            src = blk[:, :, (fj % 4) * 128:(fj % 4) * 128 + 128]
            LDQ[par].inc(e.dma_start(
                out=wqin_st[par].rearrange("p (j qq) -> p j qq", qq=128), in_=src), 16)

    @blk.tensor
    def _(e):
        e.wait_ge(VXS.h, 16)
        e.wait_ge(VTR.h, 16)
        for fj in range(32):
            par = fj % 2
            e.wait_ge(LDQ[par].h, 16 * (fj // 2 + 1))
            if fj >= 2:
                e.wait_ge(SE1.h, 4 * (fj - 1))
            grp = par * 4
            for dk in range(8):
                for tj in range(4):
                    ins = e.matmul(
                        psF1[grp + tj][:, :],
                        wqin_st[par][:, dk * 128:(dk + 1) * 128],
                        xsimT[:, dk * 2048 + tj * 512:dk * 2048 + (tj + 1) * 512],
                        start=(dk == 0), stop=(dk == 7))
                    if dk == 7 and tj == 3:
                        TF1.inc(ins)

    @blk.scalar
    def _(e):
        e.wait_ge(VCA.h, 16)
        for fj in range(32):
            par = fj % 2
            e.wait_ge(TF1.h, fj + 1)
            grp = par * 4
            if fj >= 2:
                e.wait_ge(SPL[par].h, 16 * (fj // 2))
                e.wait_ge(VAM.h, fj - 1)
            for tj in range(4):
                SE1.inc(e.activation(out=hidst[par][:, tj * 512:(tj + 1) * 512],
                                     in_=psF1[grp + tj][:, :], func=AF.Gelu,
                                     scale=invscl_in[:, fj:fj + 1],
                                     bias=bin_t[:, fj:fj + 1]))
            e.wait_ge(SE1.h, SE1.n)
            dst = hid_dram[fj * 262144:(fj + 1) * 262144].rearrange(
                "(p t) -> p t", t=2048)
            SPL[par].inc(e.dma_start(out=dst, in_=hidst[par][:]), 16)

    @blk.vector
    def _(e):
        for fj in range(32):
            e.wait_ge(SE1.h, 4 * (fj + 1))
            VAM.inc(e.tensor_reduce(out=stat[:, fj:fj + 1], in_=hidst[fj % 2][:],
                                    axis=AX.X, op=ALU.max))
        e.wait_ge(VAM.h, 32)
        VAM.inc(e.tensor_reduce(out=acc2[:], in_=stat[:, 0:32], axis=AX.X,
                                op=ALU.max))

    @blk.gpsimd
    def _(e):
        e.wait_ge(VAM.h, 33)
        GDD.inc(e.dma_start(out=bscr[2:3, :].rearrange("a p -> p a"), in_=acc2[:],
                            allow_slow_non_contiguous=True), 16)
        e.wait_ge(GDD.h, GDD.n)
        GDD.inc(e.dma_start(out=row2[:], in_=bscr[2:3, :]), 16)
        snap["gd_h2"] = GDD.n

    @blk.vector
    def _(e):
        e.wait_ge(GDD.h, snap["gd_h2"])
        HZ.inc(e.tensor_reduce(out=row1[0:1, 0:1], in_=row2[:], axis=AX.X, op=ALU.max))
        e.wait_ge(HZ.h, HZ.n)
        VDN.inc(e.tensor_scalar(out=row1[:], in0=onesr[:], scalar1=row1[0:1, 0:1],
                                scalar2=None, op0=ALU.mult))

    @blk.gpsimd
    def _(e):
        e.wait_ge(VDN.h, 2)
        GDD.inc(e.dma_start(out=agh_in[0:1, :], in_=row1[:]), 16)
        e.wait_ge(GDD.h, GDD.n)
        e.wait_ge(CC.h, 3)
        CC.inc(e.collective_compute("AllGather", ALU.bypass,
                                    replica_groups=[list(range(8))],
                                    ins=[agh_in.opt()], outs=[agh_out.opt()]))

    if PH < 23:
        _epilogue(nc, blk, x_sl, y, OUT, CC)
        _finalize()
        es.close()
        return nc

    mycore = core
    pairs = [(b, 8 + mycore) for b in range(4)] + [(b, mycore) for b in range(4)]

    def pair_plan():
        ev = []
        for qs in range(4):
            for kb in range(8 * qs + 8):
                r = kb - 8 * qs
                ev.append((qs, kb, r if r >= 0 else None))
        return ev

    PLAN = pair_plan()
    hs_sched = [list(range(0, 5)), list(range(5, 10)), list(range(10, 15)),
                list(range(15, 20)), list(range(20, 24)), list(range(24, 28)),
                list(range(28, 32))]

    lpf_cnt = [0, 0]
    pf_target = {}

    @blk.sync
    def _(e):
        e.wait_ge(CC.h, 3)
        e.wait_ge(SLN.h, 16)
        e.wait_ge(SSQ.h, 16)
        e.wait_ge(VW3.h, 4)
        e.wait_ge(TWT.h, 8)
        e.wait_ge(SXQ.h, 16)
        e.wait_ge(SAGW[0].h, 64)
        e.wait_ge(SAGW[1].h, 64)
        for pi, (bb, hh) in enumerate(pairs):
            par = pi % 2
            if pi >= 2:
                e.wait_ge(TPD.h, pi - 1)
            if pi == 1:
                e.wait_ge(VXQ.h, 16)
                e.wait_ge(TXS.h, 16)
                e.wait_ge(VAM.h, 32)
                e.wait_ge(SPL[0].h, 256)
                e.wait_ge(SPL[1].h, 256)
            if pi in (4, 5):
                LPF[par].inc(e.dma_start(out=qbuf[par][64:128, :], in_=augq_d[:, :]), 16)
                LPF[par].inc(e.dma_start(out=kbuf[par][64:128, :], in_=augk_d[:, :]), 16)
                lpf_cnt[par] += 32
            for half in range(2):
                j = 2 * bb + half
                sel = hh // 8
                qb_src = a2a1_out[j * SS1 + sel * BLKE:j * SS1 + sel * BLKE + BLKE]
                kb_src = a2a1_out[j * SS1 + 2 * BLKE + sel * BLKE:
                                  j * SS1 + 2 * BLKE + sel * BLKE + BLKE]
                vb_src = a2a1_out[j * SS1 + 4 * BLKE + sel * BLKE:
                                  j * SS1 + 4 * BLKE + sel * BLKE + BLKE]
                LPF[par].inc(e.dma_start(
                    out=qbuf[par][0:64, half * 2048:(half + 1) * 2048],
                    in_=qb_src.rearrange("(dd t) -> dd t", t=2048)), 16)
                LPF[par].inc(e.dma_start(
                    out=kbuf[par][0:64, half * 2048:(half + 1) * 2048],
                    in_=kb_src.rearrange("(dd t) -> dd t", t=2048)), 16)
                vb3 = vbuf[par].rearrange("p (k ee) -> p k ee", ee=65)
                LPF[par].inc(e.dma_start(
                    out=vb3[:, half * 16:(half + 1) * 16, 0:64],
                    in_=vb_src.rearrange("(k p dd) -> p k dd", p=128, dd=64)), 16)
                lpf_cnt[par] += 48
            pf_target[pi] = lpf_cnt[par]
            if pi == 0:
                e.wait_ge(TF1.h, 32)
                wq4 = wqout.rearrange("p (f qq) -> p f qq", qq=1024)
                for r in range(8):
                    src = agw_out[r * 1048576 + 524288:(r + 1) * 1048576].rearrange(
                        "(c p qq) -> p c qq", p=128, qq=1024)
                    LWQ.inc(e.dma_start(out=wq4[:, r * 4:(r + 1) * 4, :], in_=src), 16)
                e.wait_ge(CC.h, 4)
                LDG.inc(e.dma_start(out=hrow[:],
                                    in_=agh_out.rearrange("r p -> (r p)").rearrange(
                                        "(o n) -> o n", o=1)), 16)
            if pi >= 1:
                for fj in hs_sched[pi - 1]:
                    fpar = fj % 2
                    e.wait_ge(SPL[fpar].h, 16 * (fj // 2 + 1))
                    e.wait_ge(TF1.h, 32)
                    e.wait_ge(TVV.h, 32)
                    if fj >= 2:
                        e.wait_ge(VH.h, fj - 1)
                    src = hid_dram[fj * 262144:(fj + 1) * 262144].rearrange(
                        "(p t) -> p t", t=2048)
                    LDH[fpar].inc(e.dma_start(out=hrb[fpar][:], in_=src), 16)

    @blk.vector
    def _(e):
        e.wait_ge(VXS.h, 16)
        e.wait_ge(SPL[0].h, 256)
        e.wait_ge(SPL[1].h, 256)
        for par in range(2):
            vb3 = vbuf[par].rearrange("p (k ee) -> p k ee", ee=65)
            VBC.inc(e.memset(vb3[:, :, 64:65], 1.0))

    t_counts = dict(TSS=0, SPP=0, VMM=0, TAV=0, VN1=0, TBB=0, VN2=0)
    ss_last = {0: 0, 1: 0}

    def walk_pair_tensor(e, pi, bb, hh):
        cont = 128 if hh < 8 else 64
        gkb = 0
        vb3s = vbuf[pi % 2]
        for (qs, kb, r) in PLAN:
            par = (pi * 80 + gkb) % 2
            chs = schunks(r)
            if ss_last[par]:
                e.wait_ge(SPP.h, ss_last[par])
            for (a, b_) in chs:
                ins = e.matmul(ps_s[par][:, a:b_],
                               kbuf[pi % 2][0:cont, kb * 128:(kb + 1) * 128],
                               qbuf[pi % 2][0:cont, qs * 1024 + a:qs * 1024 + b_],
                               start=True, stop=True)
            t_counts["TSS"] += 1
            TSS.inc(ins)
            ss_last[par] = t_counts["SPP"] + 1
            t_counts["SPP"] += 1
            e.wait_ge(SPP.h, t_counts["SPP"])
            if r is not None:
                t_counts["VMM"] += 1
                e.wait_ge(VMM.h, t_counts["VMM"])
            touch = snap.setdefault(f"otouch{pi}", [False, False])
            if kb == 0:
                touch[0] = touch[1] = False
                if t_counts["VN2"]:
                    e.wait_ge(VN2.h, t_counts["VN2"])
            for (a, b_) in chs:
                bank = 0 if a < 512 else 1
                st = not touch[bank]
                touch[bank] = True
                sp = ((bank == 0 and r == 3 and a == 384)
                      or (bank == 1 and r == 7))
                ins = e.matmul(psO[:, a:b_], vb3s[:, kb * 65:kb * 65 + 65],
                               pbuf[par][:, a:b_], start=st, stop=sp)
            t_counts["TAV"] += 1
            TAV.inc(ins)
            gkb += 1
            if r == 7:
                t_counts["VN1"] += 1
                e.wait_ge(VN1.h, t_counts["VN1"])
                e.matmul(psBC[:, 0:512], onesb[0:1, 0:64], rhi_r[0:1, 0:512],
                         start=True, stop=False)
                e.matmul(psBC[:, 0:512], onesb[0:1, 0:64], rlo_r[0:1, 0:512],
                         start=False, stop=True)
                e.matmul(psBC[:, 512:1024], onesb[0:1, 0:64], rhi_r[0:1, 512:1024],
                         start=True, stop=False)
                i2 = e.matmul(psBC[:, 512:1024], onesb[0:1, 0:64], rlo_r[0:1, 512:1024],
                              start=False, stop=True)
                t_counts["TBB"] += 1
                TBB.inc(i2)
                t_counts["VN2"] += 1
        TPD.inc(e.nop())

    s_counts = dict(TSS=0, SPP=0, TAV=0)
    spb_last = {0: 0, 1: 0}

    def walk_pair_scalar(e, pi, bb, hh):
        gkb = 0
        for (qs, kb, r) in PLAN:
            par = (pi * 80 + gkb) % 2
            chs = schunks(r)
            c0 = chs[0][0]
            s_counts["TSS"] += 1
            e.wait_ge(TSS.h, s_counts["TSS"])
            if spb_last[par]:
                e.wait_ge(TAV.h, spb_last[par])
            if hh < 8:
                ins = e.activation(out=pbuf[par][:, c0:1024], in_=ps_s[par][:, c0:1024],
                                   func=AF.Exp, bias=-36.0)
            else:
                ins = e.activation(out=pbuf[par][:, c0:1024], in_=ps_s[par][:, c0:1024],
                                   func=AF.Exp, bias=chrow[:, 0:1])
            s_counts["SPP"] += 1
            SPP.inc(ins)
            s_counts["TAV"] += 1
            spb_last[par] = s_counts["TAV"]
            gkb += 1

    v_counts = dict(SPP=0, TAV=0, TBB=0, VMM=0, VN1=0, VN2=0)
    ga2_cnt = [0, 0]
    ga2_at = {}

    def ga2_occ(pi, qs):
        return 16 * (2 * pi + qs // 2 + 1)

    def walk_pair_vector(e, pi, bb, hh):
        gkb = 0
        for (qs, kb, r) in PLAN:
            par = (pi * 80 + gkb) % 2
            v_counts["SPP"] += 1
            v_counts["TAV"] += 1
            if r is not None:
                e.wait_ge(SPP.h, v_counts["SPP"])
                v_counts["VMM"] += 1
                VMM.inc(e.tensor_tensor(out=pbuf[par][:, 128 * r:128 * r + 128],
                                        in0=pbuf[par][:, 128 * r:128 * r + 128],
                                        in1=trimask[:], op=ALU.mult))
            gkb += 1
            if r == 7:
                e.wait_ge(TAV.h, v_counts["TAV"])
                hz(e, e.reciprocal(out=rden[:], in_=psO[64:65, :]))
                hz(e, e.tensor_copy(out=rhi_r[:], in_=rden[:]))
                v_counts["VN1"] += 1
                VN1.inc(e.tensor_tensor(out=rlo_r[:], in0=rden[:], in1=rhi_r[:],
                                        op=ALU.subtract))
                v_counts["TBB"] += 1
                e.wait_ge(TBB.h, v_counts["TBB"])
                prev = (pi, qs - 2) if qs >= 2 else (pi - 1, qs + 2)
                if prev[0] >= 0:
                    e.wait_ge(GA2[qs % 2].h, ga2_occ(*prev))
                hz(e, e.tensor_copy(out=bcn[:], in_=psBC[:, :]))
                v_counts["VN2"] += 1
                VN2.inc(e.tensor_tensor(out=anst[qs % 2][:, :], in0=psO[0:64, :],
                                        in1=bcn[:], op=ALU.mult))

    g_counts = dict(VN2=0)

    def walk_pair_gpsimd(e, pi, bb, hh):
        for qs in range(4):
            g_counts["VN2"] += 1
            e.wait_ge(VN2.h, g_counts["VN2"])
            j = 2 * bb + (1 if qs >= 2 else 0)
            sel = hh // 8
            off = j * SS2 + sel * BLKE
            dst = a2a2_in[off:off + BLKE].rearrange("(dd t) -> dd t", t=2048)
            GA2[qs % 2].inc(e.dma_start(
                out=dst[:, (qs % 2) * 1024:(qs % 2) * 1024 + 1024],
                in_=anst[qs % 2][:, :]), 16)
            ga2_cnt[qs % 2] += 16
            ga2_at[(pi, qs)] = ga2_cnt[qs % 2]

    for pi, (bb, hh) in enumerate(pairs):
        @blk.tensor
        def _(e, pi=pi, bb=bb, hh=hh):
            if pi == 0:
                e.wait_ge(SE1.h, 128)
                e.wait_ge(VBC.h, 3)
            e.wait_ge(LPF[pi % 2].h, pf_target[pi])
            walk_pair_tensor(e, pi, bb, hh)

        @blk.scalar
        def _(e, pi=pi, bb=bb, hh=hh):
            if pi == 0:
                e.wait_ge(SPL[0].h, 256)
                e.wait_ge(SPL[1].h, 256)
            walk_pair_scalar(e, pi, bb, hh)

        @blk.vector
        def _(e, pi=pi, bb=bb, hh=hh):
            walk_pair_vector(e, pi, bb, hh)

        @blk.gpsimd
        def _(e, pi=pi, bb=bb, hh=hh):
            walk_pair_gpsimd(e, pi, bb, hh)

        for qs in range(4):
            assert ga2_at[(pi, qs)] == ga2_occ(pi, qs), (pi, qs)

        if pi == 0:
            @blk.vector
            def _(e):
                e.wait_ge(LDG.h, 16)
                hz(e, e.tensor_reduce(out=row2[0:1, 0:1], in_=hrow[:], axis=AX.X,
                                      op=ALU.max))
                VDN.inc(e.tensor_scalar(out=row2[:], in0=onesr[:],
                                        scalar1=row2[0:1, 0:1], scalar2=None,
                                        op0=ALU.mult))

            @blk.gpsimd
            def _(e):
                e.wait_ge(VDN.h, 3)
                GDD.inc(e.dma_start(out=bscr[3:4, :], in_=row2[:]), 16)
                e.wait_ge(GDD.h, GDD.n)
                GDD.inc(e.dma_start(out=scl[:, 3:4],
                                    in_=bscr[3:4, :].rearrange("a p -> p a"),
                                    allow_slow_non_contiguous=True), 16)
                snap["gd_s2"] = GDD.n

            @blk.vector
            def _(e):
                e.wait_ge(GDD.h, snap["gd_s2"])
                hz(e, e.tensor_scalar(out=scl[:, 3:4], in0=scl[:, 3:4], scalar1=1e-8,
                                      scalar2=None, op0=ALU.max))
                hz(e, e.reciprocal(out=scl[:, 4:5], in_=scl[:, 3:4]))
                hz(e, e.tensor_scalar(out=scl[:, 4:5], in0=scl[:, 4:5],
                                      scalar1=32767.0, scalar2=None, op0=ALU.mult))
                hz(e, e.reciprocal(out=scl[:, 5:6], in_=scl[:, 4:5]))
                hz(e, e.tensor_scalar(out=gath_b[:], in0=gath_b[:], scalar1=1e-8,
                                      scalar2=None, op0=ALU.max))
                hz(e, e.tensor_scalar(out=cs_t[:], in0=gath_b[:],
                                      scalar1=float(np.float32(1.0)
                                                    / np.float32(127.0)),
                                      scalar2=None, op0=ALU.mult))
                VBC.inc(e.tensor_scalar(out=cs_t[:], in0=cs_t[:], scalar1=scl[:, 5:6],
                                        scalar2=None, op0=ALU.mult))
        else:
            chunks = hs_sched[pi - 1]

            @blk.vector
            def _(e, chunks=chunks):
                for fj in chunks:
                    fpar = fj % 2
                    e.wait_ge(LDH[fpar].h, 16 * (fj // 2 + 1))
                    if fj == 0:
                        e.wait_ge(VBC.h, 4)
                    if fj >= 2:
                        e.wait_ge(GHS[fpar].h, 16 * (fj // 2))
                    hz(e, e.tensor_scalar(out=hrb[fpar][:], in0=hrb[fpar][:],
                                          scalar1=scl[:, 4:5], scalar2=MAGIC,
                                          op0=ALU.mult, op1=ALU.add))
                    VH.inc(e.tensor_scalar(out=hqb[fpar][:], in0=hrb[fpar][:],
                                           scalar1=MAGIC, scalar2=cs_t[:, fj:fj + 1],
                                           op0=ALU.subtract, op1=ALU.mult))

            @blk.gpsimd
            def _(e, chunks=chunks):
                for fj in chunks:
                    fpar = fj % 2
                    e.wait_ge(VH.h, fj + 1)
                    dst = hsim_dram[fj * 262144:(fj + 1) * 262144].rearrange(
                        "(p t) -> p t", t=2048)
                    GHS[fpar].inc(e.dma_start(out=dst, in_=hqb[fpar][:]), 16)

    @blk.gpsimd
    def _(e):
        e.wait_ge(GA2[0].h, ga2_cnt[0])
        e.wait_ge(GA2[1].h, ga2_cnt[1])
        e.wait_ge(CC.h, 4)
        with allow_ici_all_to_all():
            CC.inc(e.collective_compute("AllToAll", ALU.bypass,
                                        replica_groups=[list(range(8))],
                                        ins=[a2a2_in.opt()], outs=[a2a2_out.opt()]))

    if PH < 24:
        _epilogue(nc, blk, x_sl, y, OUT, CC)
        _finalize()
        es.close()
        return nc

    @blk.vector
    def _(e):
        e.wait_ge(GA2[0].h, ga2_cnt[0])
        e.wait_ge(GA2[1].h, ga2_cnt[1])
        hz(e, e.tensor_copy(out=hi_r[:], in_=bor[:]))
        VBC.inc(e.tensor_tensor(out=lo_r[:], in0=bor[:], in1=hi_r[:],
                                op=ALU.subtract))

    @blk.tensor
    def _(e):
        e.wait_ge(VN2.h, 32)
        e.wait_ge(VBC.h, 5)
        e.matmul(psB[0][:, :], onesb[0:1, :], hi_r[0:1, 0:512],
                 start=True, stop=False)
        e.matmul(psB[0][:, :], onesb[0:1, :], lo_r[0:1, 0:512],
                 start=False, stop=True)
        e.matmul(psB[1][:, :], onesb[0:1, :], hi_r[0:1, 512:1024],
                 start=True, stop=False)
        TBC.inc(e.matmul(psB[1][:, :], onesb[0:1, :], lo_r[0:1, 512:1024],
                         start=False, stop=True))

    @blk.vector
    def _(e):
        e.wait_ge(TBC.h, 1)
        e.tensor_copy(out=bo_bc[:, 0:512], in_=psB[0][:, :])
        VBC.inc(e.tensor_copy(out=bo_bc[:, 512:1024], in_=psB[1][:, :]))

    @blk.sync
    def _(e):
        e.wait_ge(GHS[0].h, 256)
        e.wait_ge(GHS[1].h, 256)
        e.wait_ge(TPD.h, 8)
        e.wait_ge(GA2[0].h, ga2_cnt[0])
        e.wait_ge(GA2[1].h, ga2_cnt[1])
        for tt in range(NT):
            par = tt % 2
            if tt >= 2:
                e.wait_ge(TT2.h, tt - 1)
            src = hsim_dram.rearrange("(f p t) -> p f t", p=128, t=2048)
            LHS[par].inc(e.dma_start(
                out=hs_t[par].rearrange("p (f c) -> p f c", c=128),
                in_=src[:, :, tt * 128:(tt + 1) * 128]), 16)

    @blk.sync
    def _(e):
        e.wait_ge(CC.h, 5)
        e.wait_ge(VH.h, 32)
        e.wait_ge(GHS[0].h, 256)
        e.wait_ge(GHS[1].h, 256)
        for hh in range(16):
            src = a2a2_out[(hh % 8) * SS2 + (hh // 8) * BLKE:
                           (hh % 8) * SS2 + (hh // 8) * BLKE + BLKE]
            LAT.inc(e.dma_start(out=attT[hh // 2][64 * (hh % 2):64 * (hh % 2) + 64, :],
                                in_=src.rearrange("(dd t) -> dd t", t=2048)), 16)
        for tt in range(NT):
            e.wait_ge(TT2.h, 16)
            if tt >= 4:
                e.wait_ge(VO.h, tt - 3)
            LDX[tt % 4].inc(e.dma_start(out=x_rb2[tt % 4][:],
                                        in_=x_sl[tt * 128:(tt + 1) * 128, :]), 16)

    @blk.tensor
    def _(e):
        e.wait_ge(LWQ.h, 128)
        for tt in range(NT):
            par = tt % 2
            e.wait_ge(LHS[par].h, 16 * (tt // 2 + 1))
            if tt >= 2:
                e.wait_ge(SSI.h, 2 * (tt - 1))
            for fj in range(32):
                for half in range(2):
                    ins = e.matmul(
                        psF[par * 2 + half][:, :],
                        hs_t[par][:, fj * 128:(fj + 1) * 128],
                        wqout[:, fj * 1024 + half * 512:fj * 1024 + (half + 1) * 512],
                        start=(fj == 0), stop=(fj == 31))
                    if fj == 31 and half == 1:
                        TT2.inc(ins)

    @blk.vector
    def _(e):
        e.wait_ge(VBC.h, 6)
        for tt in range(NT):
            par = tt % 2
            e.wait_ge(TT2.h, tt + 1)
            e.tensor_tensor(out=psF[par * 2][:, :], in0=psF[par * 2][:, :],
                            in1=bo_bc[:, 0:512], op=ALU.add)
            VE2.inc(e.tensor_tensor(out=psF[par * 2 + 1][:, :],
                                    in0=psF[par * 2 + 1][:, :],
                                    in1=bo_bc[:, 512:1024], op=ALU.add))

    @blk.scalar
    def _(e):
        for tt in range(NT):
            par = tt % 2
            e.wait_ge(VE2.h, tt + 1)
            SSI.inc(e.activation(out=g_tok[:, tt * 1024:tt * 1024 + 512],
                                 in_=psF[par * 2][:, :], func=AF.Silu))
            SSI.inc(e.activation(out=g_tok[:, tt * 1024 + 512:(tt + 1) * 1024],
                                 in_=psF[par * 2 + 1][:, :], func=AF.Silu))

    @blk.tensor
    def _(e):
        e.wait_ge(VE2.h, 16)
        for dk in range(8):
            for half in range(2):
                ins = e.matmul(psB[half][0:1, :], bv16[:, dk:dk + 1],
                               wprojb[:, dk * 1024 + half * 512:
                                      dk * 1024 + (half + 1) * 512],
                               start=(dk == 0), stop=(dk == 7))
        TBC.inc(ins)

    @blk.vector
    def _(e):
        e.wait_ge(TBC.h, 2)
        e.tensor_copy(out=pcr[0:1, 0:512], in_=psB[0][0:1, :])
        hz(e, e.tensor_copy(out=pcr[0:1, 512:1024], in_=psB[1][0:1, :]))
        hz(e, e.tensor_tensor(out=pcr[:], in0=pcr[:], in1=bpr[:], op=ALU.add))
        hz(e, e.tensor_copy(out=hi_r[:], in_=pcr[:]))
        VBC.inc(e.tensor_tensor(out=lo_r[:], in0=pcr[:], in1=hi_r[:],
                                op=ALU.subtract))

    @blk.tensor
    def _(e):
        e.wait_ge(VBC.h, 7)
        e.matmul(psB[0][:, :], onesb[0:1, :], hi_r[0:1, 0:512],
                 start=True, stop=False)
        e.matmul(psB[0][:, :], onesb[0:1, :], lo_r[0:1, 0:512],
                 start=False, stop=True)
        e.matmul(psB[1][:, :], onesb[0:1, :], hi_r[0:1, 512:1024],
                 start=True, stop=False)
        TBC.inc(e.matmul(psB[1][:, :], onesb[0:1, :], lo_r[0:1, 512:1024],
                         start=False, stop=True))

    @blk.vector
    def _(e):
        e.wait_ge(TBC.h, 3)
        e.tensor_copy(out=pc_bc[:, 0:512], in_=psB[0][:, :])
        VBC.inc(e.tensor_copy(out=pc_bc[:, 512:1024], in_=psB[1][:, :]))

    @blk.tensor
    def _(e):
        e.wait_ge(LAT.h, 256)
        e.wait_ge(SSI.h, 32)
        for tt in range(NT):
            par = tt % 2
            if tt >= 2:
                e.wait_ge(VPJ.h, tt - 1)
            for dk in range(8):
                for half in range(2):
                    ins = e.matmul(
                        psF[par * 2 + half][:, :],
                        attT[dk][:, tt * 128:(tt + 1) * 128],
                        wprojb[:, dk * 1024 + half * 512:dk * 1024 + (half + 1) * 512],
                        start=(dk == 0), stop=(dk == 7))
                    if dk == 7 and half == 1:
                        TPJ.inc(ins)

    @blk.vector
    def _(e):
        e.wait_ge(VBC.h, 8)
        for tt in range(NT):
            par = tt % 2
            e.wait_ge(TPJ.h, tt + 1)
            e.tensor_tensor(out=s_tok[:, tt * 1024:tt * 1024 + 512],
                            in0=psF[par * 2][:, :], in1=pc_bc[:, 0:512], op=ALU.add)
            VPJ.inc(e.tensor_tensor(out=s_tok[:, tt * 1024 + 512:(tt + 1) * 1024],
                                    in0=psF[par * 2 + 1][:, :], in1=pc_bc[:, 512:1024],
                                    op=ALU.add))

    @blk.vector
    def _(e):
        for tt in range(NT):
            e.wait_ge(VPJ.h, tt + 1)
            e.wait_ge(SSI.h, 2 * (tt + 1))
            e.wait_ge(LDX[tt % 4].h, 16 * (8 + tt // 4 + 1))
            if tt >= 2:
                e.wait_ge(SO[tt % 2].h, 16 * (tt // 2))
            hz(e, e.tensor_tensor(out=s_tok[:, tt * 1024:(tt + 1) * 1024],
                                  in0=s_tok[:, tt * 1024:(tt + 1) * 1024],
                                  in1=g_tok[:, tt * 1024:(tt + 1) * 1024],
                                  op=ALU.mult))
            VO.inc(e.tensor_tensor(out=ostage[tt % 2][:],
                                   in0=s_tok[:, tt * 1024:(tt + 1) * 1024],
                                   in1=x_rb2[tt % 4][:], op=ALU.add))

    @blk.scalar
    def _(e):
        for tt in range(NT):
            e.wait_ge(VO.h, tt + 1)
            SO[tt % 2].inc(e.dma_start(out=y[tt * 128:(tt + 1) * 128, :],
                                       in_=ostage[tt % 2][:]), 16)

    @blk.gpsimd
    def _(e):
        e.wait_ge(SO[0].h, 128)
        e.wait_ge(SO[1].h, 128)
        e.wait_ge(CC.h, 5)
        if TAPS:
            for nm, (dst, src) in tout.items():
                if len(src.tensor.shape) == 1:
                    n = int(src.tensor.shape[0])
                    GDD.inc(e.dma_start(
                        out=dst.rearrange("(a b) -> a b", b=n // 64),
                        in_=src.rearrange("(a b) -> a b", b=n // 64)), 16)
                else:
                    GDD.inc(e.dma_start(out=dst[:, :], in_=src[:, :]), 16)
            e.wait_ge(GDD.h, GDD.n)

    @blk.sync
    def _(e):
        e.wait_ge(SO[0].h, 128)
        e.wait_ge(SO[1].h, 128)

    _finalize()
    es.close()
    return nc


_CACHE = {}


def _host_constants():
    if "aug" in _CACHE:
        return _CACHE["aug"]
    dim = 512
    pos = np.arange(dim, dtype=np.float64)
    freqs = 1.0 / (10000.0 ** (pos / dim))
    i_idx = np.arange(T, dtype=np.float64)
    theta = i_idx[:, None] * freqs[None, :256]
    cos_t = np.cos(theta)
    sin_t = np.sin(theta)
    A = (cos_t + sin_t).astype(np.float32)
    Bf = (sin_t - cos_t).astype(np.float32)
    augq = np.zeros((8, 64, T), dtype=BF)
    augk = np.zeros((8, 64, T), dtype=BF)
    for h in range(8):
        fs = slice(32 * h, 32 * h + 32)
        augq[h, 0:32] = A.T[fs].astype(BF)
        augq[h, 32:64] = Bf.T[fs].astype(BF)
        augk[h, 0:32] = cos_t.T[fs].astype(np.float32).astype(BF)
        augk[h, 32:64] = sin_t.T[fs].astype(np.float32).astype(BF)
    tri = (np.arange(128)[:, None] <= np.arange(128)[None, :]).astype(BF)
    idn = np.eye(128, dtype=BF)
    _CACHE["aug"] = (augq, augk, np.ascontiguousarray(tri), idn)
    return _CACHE["aug"]


def kernel(L0_attn_x, L0_attn_w_qkv, L0_attn_b_qkv, L0_attn_w_proj, L0_attn_b_proj,
           L0_attn_pos_independent, L0_ffn_W_in, L0_ffn_b_in, L0_ffn_W_out,
           L0_ffn_b_out):
    augq, augk, tri, idn = _host_constants()
    x2 = np.ascontiguousarray(L0_attn_x.reshape(16384, D))
    w_qkvT = np.ascontiguousarray(L0_attn_w_qkv.T)
    w_projT = np.ascontiguousarray(L0_attn_w_proj.T)
    W_inT = np.ascontiguousarray(L0_ffn_W_in.T)
    W_out = np.ascontiguousarray(L0_ffn_W_out)
    b_qkv_t = np.ascontiguousarray(L0_attn_b_qkv.reshape(24, 128).T)
    b_in_t = np.ascontiguousarray(L0_ffn_b_in.reshape(32, 128).T)
    b_out_row = np.ascontiguousarray(L0_ffn_b_out.reshape(1, D))
    b_proj_row = np.ascontiguousarray(L0_attn_b_proj.reshape(1, D))
    pi_flat = np.ascontiguousarray(L0_attn_pos_independent.reshape(-1))

    if "nc" not in _CACHE:
        with allow_ici_all_to_all():
            _CACHE["nc"] = build(0)
    nc = _CACHE["nc"]

    in_maps = []
    for c in range(8):
        in_maps.append({
            "x_sl": np.ascontiguousarray(x2[c * TOK:(c + 1) * TOK]),
            "w_qkvT": w_qkvT,
            "w_projT": w_projT,
            "w_inT_sl": np.ascontiguousarray(W_inT[c * 512:(c + 1) * 512]),
            "w_out_sl": np.ascontiguousarray(W_out[c * 512:(c + 1) * 512]),
            "b_qkv_t": b_qkv_t,
            "b_in_t": b_in_t,
            "b_out_row": b_out_row,
            "b_proj_row": b_proj_row,
            "pi_sl": np.ascontiguousarray(pi_flat[64 * c:64 * c + 64].reshape(1, 64)),
            "augq": np.ascontiguousarray(augq[c]),
            "augk": np.ascontiguousarray(augk[c]),
            "trimask": tri,
            "ident": idn,
            "identf": np.eye(128, dtype=np.float32),
        })
    res = run_bass_kernel_spmd(nc, in_maps, core_ids=list(range(8)))
    out = np.concatenate([res.results[c]["y"] for c in range(8)], axis=0)
    return out.reshape(4, T, D)
